# version 01 (final)
# baseline (speedup 1.0000x reference)
import math
import contextlib
import numpy as np
import ml_dtypes
import concourse.bass as bass
import concourse.mybir as mybir
from concourse.bass_utils import run_bass_kernel_spmd

F32 = mybir.dt.float32
BF16 = mybir.dt.bfloat16
I32 = mybir.dt.int32
AF = mybir.ActivationFunctionType
ALU = mybir.AluOpType
AX = mybir.AxisListType
bfdt = ml_dtypes.bfloat16

D = 1024
NCH = 8
T = 4096
SEQ = 2048
NTB = 8
TBW = 512
H0 = 3072
H1 = 2816
NH = 16
DH = 64
CSHIFT = 20.0
MAGIC = 12582912.0
EPS = 1e-5
INV2PI = float(1.0 / (2.0 * math.pi))
TWOPI = float(2.0 * math.pi)
Q127 = float(1.0 / 127.0)

ENGS = ("sync", "tensor", "scalar", "vector", "gpsimd")


class P:
    def __init__(self, nc, stack):
        self.nc = nc
        self.prog = []
        self.sems = {}
        self.cnt = {}
        self.stack = stack

    def sem(self, name):
        self.sems[name] = self.stack.enter_context(self.nc.semaphore(name))
        self.cnt[name] = 0

    def on(self, eng, fn):
        self.prog.append((eng, fn))

    def wait(self, eng, sem, val=None):
        v = self.cnt[sem] if val is None else val
        if v <= 0:
            return
        h = self.sems[sem]
        self.on(eng, lambda e, h=h, v=v: e.wait_ge(h, v))

    def emit(self, eng, fn, inc=None, dma=False):
        incs = []
        if inc is not None:
            for s in ([inc] if isinstance(inc, str) else inc):
                by = 16 if dma else 1
                self.cnt[s] += by
                incs.append((self.sems[s], by))

        def run(e, fn=fn, incs=incs):
            inst = fn(e)
            for h, by in incs:
                inst.then_inc(h, by)
        self.on(eng, run)

    def val(self, sem):
        return self.cnt[sem]

    def replay(self, block):
        progs = {k: [f for en, f in self.prog if en == k] for k in ENGS}

        @block.sync
        def _(e):
            for f in progs["sync"]:
                f(e)

        @block.tensor
        def _(e):
            for f in progs["tensor"]:
                f(e)

        @block.scalar
        def _(e):
            for f in progs["scalar"]:
                f(e)

        @block.vector
        def _(e):
            for f in progs["vector"]:
                f(e)

        @block.gpsimd
        def _(e):
            for f in progs["gpsimd"]:
                f(e)


def build(debug=0, phases=99):
    nc = bass.Bass("TRN2", target_bir_lowering=False)

    def din(name, shape, dt=F32):
        return nc.dram_tensor(name, shape, dt, kind="ExternalInput").ap()

    x0T = din("x0T", [D, T])
    wqkv0T = din("wqkv0T", [D, 3 * D])
    bqkv0 = din("bqkv0", [128, 24])
    wo0T = din("wo0T", [D, D])
    bo0 = din("bo0", [128, 8])
    alpha0 = din("alpha0", [1, NH])
    wg0T = din("wg0T", [D, H0], I32)
    wu0T = din("wu0T", [D, H0], I32)
    wd0T = din("wd0T", [H0, D], I32)
    wq1T = din("wq1T", [D, D])
    wk1T = din("wk1T", [D, D])
    wv1T = din("wv1T", [D, D])
    wo1T = din("wo1T", [D, D])
    bq1 = din("bq1", [128, 8])
    bk1 = din("bk1", [128, 8])
    bv1 = din("bv1", [128, 8])
    bo1 = din("bo1", [128, 8])
    clipx = din("clipx", [1, 1])
    clipd = din("clipd", [1, 1])
    wg1n = din("wg1n", [H1, D])
    wu1n = din("wu1n", [H1, D])
    wd1n = din("wd1n", [D, H1])

    yT = nc.dram_tensor("yT", [D, T], F32, kind="ExternalOutput").ap()

    sk = "ExternalOutput" if debug >= 2 else "Internal"
    qspill = nc.dram_tensor("qspill", [D, T], BF16, kind=sk).ap()
    kspill = nc.dram_tensor("kspill", [D, T], BF16, kind=sk).ap()
    vspill = nc.dram_tensor("vspill", [T, NH * 65], BF16, kind=sk).ap()
    h0sp = nc.dram_tensor("h0sp", [H0, T], BF16, kind=sk).ap()
    h1sp = nc.dram_tensor("h1sp", [H1, T], F32, kind="Internal").ap()
    wdbf0 = nc.dram_tensor("wdbf0", [H0, D], BF16, kind="Internal").ap()
    cwtg = nc.dram_tensor("cwtg", [H1, D], BF16, kind="Internal").ap()
    cwtu = nc.dram_tensor("cwtu", [H1, D], BF16, kind="Internal").ap()
    cwtd = nc.dram_tensor("cwtd", [H1, D], BF16, kind="Internal").ap()
    ccin1 = nc.dram_tensor("ccin1", [1, 8], F32, kind="Internal").ap()
    ccout1 = nc.dram_tensor("ccout1", [1, 8], F32, kind="Internal").ap()
    ccin2 = nc.dram_tensor("ccin2", [1, 8], F32, kind="Internal").ap()
    ccout2 = nc.dram_tensor("ccout2", [1, 8], F32, kind="Internal").ap()

    dbg = {}
    if debug >= 2:
        dbg["dbg_tabs"] = nc.dram_tensor("dbg_tabs", [256, SEQ], BF16,
                                         kind="ExternalOutput").ap()
    if debug >= 3:
        dbg["dbg_srow"] = nc.dram_tensor("dbg_srow", [128, 896], BF16,
                                         kind="ExternalOutput").ap()
    if debug >= 1:
        for nm in ("dbg_resid0", "dbg_xhat0", "dbg_resid1", "dbg_xhat1",
                   "dbg_c1", "dbg_att0", "dbg_att1", "dbg_resid2"):
            dbg[nm] = nc.dram_tensor(nm, [D, T], BF16,
                                     kind="ExternalOutput").ap()

    with contextlib.ExitStack() as stack:
        e = stack.enter_context
        p = P(nc, stack)

        resid = e(nc.sbuf_tensor("resid", [128, NCH * T], BF16))
        slot = e(nc.sbuf_tensor("slot", [128, 33024], BF16))
        VOFF = NCH * SEQ
        attb = e(nc.sbuf_tensor("attb", [128, NCH * SEQ], BF16))
        ctab = e(nc.sbuf_tensor("ctab", [128, SEQ], BF16))
        stab = e(nc.sbuf_tensor("stab", [128, SEQ], BF16))
        dstrip = e(nc.sbuf_tensor("dstrip", [128, 3968], BF16))
        masks = e(nc.sbuf_tensor("masks", [128, 896], BF16))
        pswap = e(nc.sbuf_tensor("pswap", [128, 128], BF16))
        ident = e(nc.sbuf_tensor("ident", [128, 128], BF16))
        wsi = e(nc.sbuf_tensor("wsi", [128, 128], I32))
        wsf = e(nc.sbuf_tensor("wsf", [128, 512], F32))
        wbf_a = e(nc.sbuf_tensor("wbf_a", [128, 1024], BF16))
        wbf_b = e(nc.sbuf_tensor("wbf_b", [128, 1024], BF16))
        wbf_c = e(nc.sbuf_tensor("wbf_c", [128, 256], BF16))
        expb = e(nc.sbuf_tensor("expb", [128, 1024], BF16))
        qstg = e(nc.sbuf_tensor("qstg", [128, TBW], BF16))
        sstage = e(nc.sbuf_tensor("sstage", [128, 1024], F32))
        hstage = e(nc.sbuf_tensor("hstage", [128, 1024], F32))
        spst = e(nc.sbuf_tensor("spst", [128, 2 * TBW], BF16))
        vstg = e(nc.sbuf_tensor("vstg", [128, 1040], BF16))
        srow = e(nc.sbuf_tensor("srow", [128, 1024], F32))
        epsT = e(nc.sbuf_tensor("epsT", [128, 1], F32))
        nrm = e(nc.sbuf_tensor("nrm", [64, TBW], BF16))
        onesb = e(nc.sbuf_tensor("onesb", [128, 1], BF16))
        onesf = e(nc.sbuf_tensor("onesf", [128, 128], F32))
        biasC = e(nc.sbuf_tensor("biasC", [128, 1], F32))
        bstore = e(nc.sbuf_tensor("bstore", [128, 80], F32))
        bbf = e(nc.sbuf_tensor("bbf", [128, 16], BF16))
        abc8 = e(nc.sbuf_tensor("abc8", [128, NH], F32))
        wsfm = e(nc.sbuf_tensor("wsfm", [128, 64], F32))
        invf = e(nc.sbuf_tensor("invf", [128, 4], F32))
        amax1 = e(nc.sbuf_tensor("amax1", [128, 1], F32))
        amax2 = e(nc.sbuf_tensor("amax2", [128, 1], F32))
        vtt = e(nc.sbuf_tensor("vtt", [128, 32], F32))

        pA = e(nc.psum_tensor("pA", [128, 1024], F32))
        pB = e(nc.psum_tensor("pB", [128, 1024], F32))
        pC = e(nc.psum_tensor("pC", [128, 512], F32))
        pD = e(nc.psum_tensor("pD", [128, 512], F32))
        pE = e(nc.psum_tensor("pE", [128, 512], F32))
        pF = e(nc.psum_tensor("pF", [128, 512], F32))

        for s in ("ldw", "ldg", "mm", "mq", "evv", "evs", "evg", "stw",
                  "stg", "cc"):
            p.sem(s)

        blk = e(nc.Block())

        psum_last = {}

        def pwait(ps, eng="tensor"):
            k = id(ps)
            if k in psum_last:
                sem, v = psum_last[k]
                p.wait(eng, sem, v)

        def pmark(ps, sem):
            psum_last[id(ps)] = (sem, p.val(sem))

        def MM(out_fn, lhs_fn, rhs_fn, start, stop, inc=None, tp=None):
            p.emit("tensor", lambda E: E.matmul(
                out_fn(), lhs_fn(), rhs_fn(), start=start, stop=stop,
                **({"tile_position": tp} if tp else {})), inc=inc)

        def rchunk(c, tb):
            return resid[:, c * T + tb * TBW: c * T + (tb + 1) * TBW]

        def schunk(c, tb):
            return slot[:, c * T + tb * TBW: c * T + (tb + 1) * TBW]

        for fn in [
            lambda E: E.dma_start(out=bstore[:, 0:24], in_=bqkv0),
            lambda E: E.dma_start(out=bstore[:, 24:32], in_=bo0),
            lambda E: E.dma_start(out=bstore[:, 32:40], in_=bq1),
            lambda E: E.dma_start(out=bstore[:, 40:48], in_=bk1),
            lambda E: E.dma_start(out=bstore[:, 48:56], in_=bv1),
            lambda E: E.dma_start(out=bstore[:, 56:64], in_=bo1),
            lambda E: E.dma_start(out=srow[0:1, 0:NH], in_=alpha0),
            lambda E: E.dma_start(out=srow[32:33, 4:5], in_=clipx),
            lambda E: E.dma_start(out=srow[32:33, 5:6], in_=clipd),
        ]:
            p.emit("sync", fn, inc="ldw", dma=True)

        for pc in range(4):
            w = min(1024, 3968 - pc * 1024)
            p.emit("gpsimd", lambda E, pc=pc, w=w: E.iota(
                sstage[:, 0:w], pattern=[[-1, w]], base=1920 - pc * 1024,
                channel_multiplier=1, allow_small_or_imprecise_dtypes=True),
                inc="evg")
            p.wait("vector", "evg")
            p.emit("vector", lambda E, pc=pc, w=w: E.scalar_tensor_tensor(
                dstrip[:, pc * 1024: pc * 1024 + w], sstage[:, 0:w], -1.0,
                sstage[:, 0:w], ALU.mult, ALU.max), inc="evv")
            p.wait("gpsimd", "evv")

        p.emit("gpsimd", lambda E: E.iota(
            sstage[:, 0:896], pattern=[[1, 896]], base=-384,
            channel_multiplier=-1, allow_small_or_imprecise_dtypes=True),
            inc="evg")
        p.wait("vector", "evg")
        p.emit("vector", lambda E: E.tensor_scalar(
            masks[:, 0:896], sstage[:, 0:896], 0.0, None, ALU.is_ge),
            inc="evv")
        p.wait("gpsimd", "evv")

        p.emit("vector", lambda E: E.memset(pswap[:], 0.0), inc="evv")
        for base, row0 in ((32, 0), (-32, 32), (32, 64), (-32, 96)):
            p.emit("gpsimd", lambda E, base=base: E.iota(
                sstage[:, 0:128], pattern=[[-1, 128]], base=base,
                channel_multiplier=1, allow_small_or_imprecise_dtypes=True),
                inc="evg")
            p.wait("vector", "evg")
            p.emit("vector", lambda E, row0=row0: E.tensor_scalar(
                pswap[row0:row0 + 32, :], sstage[row0:row0 + 32, 0:128], 0.0,
                None, ALU.is_equal), inc="evv")
            p.wait("gpsimd", "evv")
        p.emit("gpsimd", lambda E: E.iota(
            sstage[:, 0:128], pattern=[[-1, 128]], base=0,
            channel_multiplier=1, allow_small_or_imprecise_dtypes=True),
            inc="evg")
        p.wait("vector", "evg")
        p.emit("vector", lambda E: E.tensor_scalar(
            ident[:, :], sstage[:, 0:128], 0.0, None, ALU.is_equal), inc="evv")
        p.wait("gpsimd", "evv")

        p.emit("gpsimd", lambda E: E.iota(
            invf[0:32, 2:3], pattern=[[1, 1]], base=0, channel_multiplier=1,
            allow_small_or_imprecise_dtypes=True), inc="evg")
        for dst in (32, 64, 96):
            p.wait("gpsimd", "evg")
            p.emit("gpsimd", lambda E, dst=dst: E.dma_start(
                out=invf[dst:dst + 32, 2:3], in_=invf[0:32, 2:3]),
                inc="ldg", dma=True)


        for ps_ in "ABCDEF":
            pass
        p.emit("vector", lambda E: E.memset(pA[:], 0.0), inc="evv")
        p.emit("vector", lambda E: E.memset(pB[:], 0.0), inc="evv")
        p.emit("vector", lambda E: E.memset(pC[:], 0.0), inc="evv")
        p.emit("vector", lambda E: E.memset(pD[:], 0.0), inc="evv")
        p.emit("vector", lambda E: E.memset(pE[:], 0.0), inc="evv")
        p.emit("vector", lambda E: E.memset(pF[:], 0.0), inc="evv")
        p.emit("vector", lambda E: E.memset(onesb[:], 1.0), inc="evv")
        p.emit("vector", lambda E: E.memset(onesf[:], 1.0), inc="evv")
        p.emit("vector", lambda E: E.memset(biasC[:], -CSHIFT), inc="evv")
        p.emit("vector", lambda E: E.memset(epsT[:], EPS), inc="evv")
        p.emit("vector", lambda E: E.memset(vtt[:], 0.0), inc="evv")
        p.emit("vector", lambda E: E.memset(
            vstg[:].rearrange("p (h e) -> p h e", h=NH)[:, :, 64:65],
            1.0), inc="evv")
        p.emit("vector", lambda E: E.memset(invf[0:32, 1:2], -1.0), inc="evv")
        p.emit("vector", lambda E: E.memset(invf[32:64, 1:2], 1.0), inc="evv")
        p.emit("vector", lambda E: E.memset(invf[64:96, 1:2], -1.0), inc="evv")
        p.emit("vector", lambda E: E.memset(invf[96:128, 1:2], 1.0), inc="evv")

        p.wait("scalar", "ldw")
        p.wait("scalar", "ldg")
        p.wait("scalar", "evg")
        p.emit("scalar", lambda E: E.activation(
            srow[32:33, 0:1], srow[32:33, 4:5], AF.Sigmoid), inc="evs")
        p.emit("scalar", lambda E: E.activation(
            srow[32:33, 1:2], srow[32:33, 5:6], AF.Sigmoid), inc="evs")
        p.emit("scalar", lambda E: E.activation(
            srow[0:1, 16:32], srow[0:1, 0:NH], AF.Abs), inc="evs")
        p.emit("scalar", lambda E: E.activation(
            srow[0:1, 16:32], srow[0:1, 16:32], AF.Exp, scale=-1.0),
            inc="evs")
        p.wait("vector", "evs")
        p.emit("vector", lambda E: E.tensor_add(
            srow[0:1, 16:32], srow[0:1, 16:32], onesf[0:1, 0:16]), inc="evv")
        p.wait("scalar", "evv")
        p.emit("scalar", lambda E: E.activation(
            srow[0:1, 16:32], srow[0:1, 16:32], AF.Ln), inc="evs")
        p.emit("scalar", lambda E: E.activation(
            srow[0:1, 32:48], srow[0:1, 0:NH], AF.Relu), inc="evs")
        p.wait("vector", "evs")
        p.emit("vector", lambda E: E.tensor_add(
            srow[0:1, 48:64], srow[0:1, 16:32], srow[0:1, 32:48]), inc="evv")
        p.emit("scalar", lambda E: E.activation(
            invf[:, 0:1], invf[:, 2:3], AF.Exp,
            scale=float(-math.log(10000.0) / 32.0)), inc="evs")
        p.wait("tensor", "evv")
        p.wait("tensor", "evs")
        pwait(pD)
        MM(lambda: pD[0:128, 0:NH], lambda: onesf[0:1, 0:128],
           lambda: srow[0:1, 48:64], True, True, inc="mm")
        p.wait("vector", "mm")
        p.emit("vector", lambda E: E.tensor_scalar(
            abc8[:, :], pD[:, 0:NH], -8.0, None, ALU.mult), inc="evv")
        pmark(pD, "evv")

        for half in (0, 1):
            p.wait("gpsimd", "evv")
            p.wait("gpsimd", "evs")
            p.emit("gpsimd", lambda E, half=half: E.iota(
                sstage[:, 0:1024], pattern=[[1, 1024]], base=half * 1024,
                channel_multiplier=0, allow_small_or_imprecise_dtypes=True),
                inc="evg")
            p.wait("scalar", "evg")
            p.emit("scalar", lambda E: E.activation(
                hstage[:, 0:1024], sstage[:, 0:1024], AF.Copy,
                scale=invf[:, 0:1]), inc="evs")
            p.wait("vector", "evs")
            p.emit("vector", lambda E: E.tensor_scalar(
                hstage[:, 0:1024], hstage[:, 0:1024], INV2PI, None, ALU.mult),
                inc="evv")
            for tab, off in ((stab, 0.0), (ctab, 0.25)):
                for pc in (0, 1):
                    sg = lambda pc=pc: sstage[:, pc * 512:(pc + 1) * 512]
                    p.wait("vector", "evs")
                    p.emit("vector", lambda E, off=off, pc=pc, sg=sg:
                           E.tensor_scalar(
                               sg(),
                               hstage[:, pc * 512:(pc + 1) * 512],
                               off + MAGIC, -MAGIC, ALU.add, ALU.add),
                           inc="evv")
                    p.emit("vector", lambda E, off=off, sg=sg:
                           E.tensor_scalar(
                               sg(), sg(), -1.0, off,
                               ALU.mult, ALU.add), inc="evv")
                    p.emit("vector", lambda E, pc=pc, sg=sg: E.tensor_add(
                        sg(), sg(),
                        hstage[:, pc * 512:(pc + 1) * 512]), inc="evv")
                    p.emit("vector", lambda E, sg=sg: E.tensor_scalar(
                        sg(), sg(), TWOPI, None, ALU.mult),
                        inc="evv")
                    p.wait("scalar", "evv")
                    p.emit("scalar", lambda E, tab=tab, half=half, pc=pc,
                           sg=sg: E.activation(
                               tab[:, half * 1024 + pc * 512:
                                   half * 1024 + (pc + 1) * 512],
                               sg(), AF.Sin), inc="evs")
        p.wait("vector", "evs")
        p.emit("vector", lambda E: E.tensor_scalar(
            stab[:, :], stab[:, :], invf[:, 1:2], None, ALU.mult), inc="evv")
        p.emit("vector", lambda E: E.tensor_copy(
            bbf[:, 0:8], bstore[:, 16:24]), inc="evv")
        p.emit("vector", lambda E: E.tensor_copy(
            bbf[:, 8:16], bstore[:, 48:56]), inc="evv")

        for c in range(NCH):
            for tb in range(NTB):
                half = tb % 2
                p.wait("sync", "evv")
                p.wait("sync", "evs")
                p.emit("sync", lambda E, c=c, tb=tb, half=half: E.dma_start(
                    out=sstage[:, half * 512:(half + 1) * 512],
                    in_=x0T[c * 128:(c + 1) * 128, tb * TBW:(tb + 1) * TBW]),
                    inc="ldw", dma=True)
                p.wait("vector", "ldw")
                p.emit("vector", lambda E, c=c, tb=tb, half=half:
                       E.tensor_copy(
                           rchunk(c, tb),
                           sstage[:, half * 512:(half + 1) * 512]), inc="evv")

        if debug >= 1:
            p.wait("sync", "evv")
            for c in range(NCH):
                p.emit("sync", lambda E, c=c: E.dma_start(
                    out=dbg["dbg_resid0"][c * 128:(c + 1) * 128, :],
                    in_=resid[:, c * T:(c + 1) * T]), inc="stw", dma=True)

        def ln_stats(tb):
            pwait(pE)
            pwait(pF)
            pwait(pC)
            p.wait("tensor", "evv")
            for c in range(NCH):
                MM(lambda: pE[0:1, :], lambda: onesb[:, 0:1],
                   lambda c=c, tb=tb: rchunk(c, tb), c == 0, c == NCH - 1,
                   inc="mm" if c == NCH - 1 else None)
            slots = [expb[:, 0:512], expb[:, 512:1024],
                     spst[:, 0:512], spst[:, 512:1024]]
            gA = None
            p.wait("scalar", "stg")
            p.wait("scalar", "evv")
            for c in range(NCH):
                if c == 4:
                    p.wait("scalar", "mm", gA)
                p.emit("scalar", lambda E, c=c, tb=tb: E.activation(
                    slots[c % 4], rchunk(c, tb), AF.Square), inc="evs")
                p.wait("tensor", "evs")
                tgt = (lambda: pF[0:1, :]) if c < 4 else (lambda: pC[0:1, :])
                MM(tgt, lambda: onesb[:, 0:1],
                   lambda c=c: slots[c % 4], c % 4 == 0, c % 4 == 3,
                   inc="mm" if c % 4 == 3 else None)
                if c == 3:
                    gA = p.val("mm")
            p.wait("vector", "mm")
            p.wait("scalar", "mm")
            p.emit("scalar", lambda E: E.activation(
                srow[0:1, 0:512], pE[0:1, :], AF.Identity,
                scale=float(1.0 / D)), inc="evs")
            p.emit("vector", lambda E: E.tensor_copy(
                srow[0:1, 512:1024], pC[0:1, :]), inc="evv")
            p.emit("vector", lambda E: E.tensor_add(
                srow[0:1, 512:1024], pF[0:1, :], srow[0:1, 512:1024]),
                inc="evv")
            pmark(pE, "evs")
            pmark(pF, "evv")
            pmark(pC, "evv")
            p.wait("scalar", "evv")
            p.emit("scalar", lambda E: E.activation(
                srow[0:1, 512:1024], srow[0:1, 512:1024], AF.Identity,
                scale=float(1.0 / D), bias=epsT[0:1, 0:1]), inc="evs")
            p.wait("vector", "evs")
            p.emit("vector", lambda E: E.tensor_mul(
                sstage[0:1, 0:512], srow[0:1, 0:512], srow[0:1, 0:512]),
                inc="evv")
            p.emit("vector", lambda E: E.scalar_tensor_tensor(
                srow[0:1, 512:1024], sstage[0:1, 0:512], -1.0,
                srow[0:1, 512:1024], ALU.mult, ALU.add), inc="evv")
            p.wait("scalar", "evv")
            p.emit("scalar", lambda E: E.activation(
                srow[0:1, 512:1024], srow[0:1, 512:1024], AF.Ln), inc="evs")
            p.emit("scalar", lambda E: E.activation(
                srow[0:1, 512:1024], srow[0:1, 512:1024], AF.Exp,
                scale=-0.5), inc="evs")

        def bcast_row(c0, ps):
            p.wait("tensor", "evs")
            p.wait("tensor", "evv")
            pwait(ps)
            MM(lambda ps=ps: ps[0:128, 0:512],
               lambda: onesf[0:1, 0:128],
               lambda c0=c0: srow[0:1, c0:c0 + 512], True, True, inc="mm")

        def xhat_tb(tb, collect_max=False, first=False):
            ln_stats(tb)
            bcast_row(0, pC)
            bcast_row(512, pD)
            p.wait("vector", "mm")
            for c in range(NCH):
                p.emit("vector", lambda E, c=c, tb=tb: E.tensor_sub(
                    sstage[:, 0:512], rchunk(c, tb), pC[:, 0:512]), inc="evv")
                if not collect_max:
                    p.emit("vector", lambda E, c=c, tb=tb: E.tensor_mul(
                        schunk(c, tb), sstage[:, 0:512], pD[:, 0:512]),
                        inc="evv")
                else:
                    p.emit("vector", lambda E: E.tensor_mul(
                        sstage[:, 512:1024], sstage[:, 0:512], pD[:, 0:512]),
                        inc="evv")
                    p.emit("vector", lambda E, c=c, tb=tb: E.tensor_copy(
                        schunk(c, tb), sstage[:, 512:1024]), inc="evv")
                    p.emit("vector", lambda E: E.tensor_reduce(
                        vtt[:, 0:1], sstage[:, 512:1024], AX.X, ALU.max,
                        apply_absolute_value=True), inc="evv")
                    if first and c == 0:
                        p.emit("vector", lambda E: E.tensor_copy(
                            amax1[:, 0:1], vtt[:, 0:1]), inc="evv")
                    else:
                        p.emit("vector", lambda E: E.tensor_max(
                            amax1[:, 0:1], amax1[:, 0:1], vtt[:, 0:1]),
                            inc="evv")
            pmark(pC, "evv")
            pmark(pD, "evv")

        if phases >= 1:
            for tb in range(NTB):
                xhat_tb(tb)
                if debug >= 3 and tb == 0:
                    p.emit("vector", lambda E: E.tensor_copy(
                        masks[0:1, 0:512], srow[0:1, 0:512]), inc="evv")
                    p.emit("vector", lambda E: E.tensor_copy(
                        masks[0:1, 512:896], srow[0:1, 512:896]), inc="evv")
                    p.wait("sync", "evv")
                    p.emit("sync", lambda E: E.dma_start(
                        out=dbg["dbg_srow"], in_=masks[:, :]),
                        inc="stw", dma=True)
            if debug >= 1:
                p.wait("sync", "evv")
                for c in range(NCH):
                    p.emit("sync", lambda E, c=c: E.dma_start(
                        out=dbg["dbg_xhat0"][c * 128:(c + 1) * 128, :],
                        in_=slot[:, c * T:(c + 1) * T]), inc="stw", dma=True)

        def wload_T_f32(src, oc, buf):
            for hh in range(2):
                def fn(E, hh=hh):
                    sl = src[hh * 512:(hh + 1) * 512,
                             oc * 128:(oc + 1) * 128]
                    s3 = sl.rearrange("(k p) c -> p k c", p=128)
                    d3 = wsf[:, 0:512].rearrange("p (k c) -> p k c", k=4)
                    return E.dma_start(out=d3, in_=s3)
                p.wait("sync", "evg")
                p.wait("sync", "evv")
                p.emit("sync", fn, inc="ldw", dma=True)
                p.wait("gpsimd", "ldw")
                p.emit("gpsimd", lambda E, buf=buf, hh=hh: E.tensor_copy(
                    buf[:, hh * 512:(hh + 1) * 512], wsf[:, 0:512]),
                    inc="evg")

        def wload_T_i32(src, oc, buf):
            for hh in range(8):
                def fn(E, hh=hh):
                    sl = src[hh * 128:(hh + 1) * 128,
                             oc * 128:(oc + 1) * 128]
                    return E.dma_start(out=wsi[:, 0:128], in_=sl)
                p.wait("sync", "evg")
                p.emit("sync", fn, inc="ldw", dma=True)
                p.wait("gpsimd", "ldw")
                p.emit("gpsimd", lambda E, buf=buf, hh=hh: E.tensor_copy(
                    buf[:, hh * 128:(hh + 1) * 128], wsi[:, 0:128]),
                    inc="evg")

        spst_busy = {}

        def evac_spill(dst, bias_col, ocd, tb, ps):
            sl = tb % 2
            p.wait("scalar", "mm")
            if sl in spst_busy:
                p.wait("scalar", "stg", spst_busy[sl])
            s = spst[:, sl * 512:(sl + 1) * 512]
            p.emit("scalar", lambda E, ps=ps, b=bias_col, s=s: E.activation(
                s, ps[:, 0:512], AF.Identity, bias=b), inc="evs")
            pmark(ps, "evs")
            p.wait("gpsimd", "evs")
            p.emit("gpsimd", lambda E, ocd=ocd, tb=tb, s=s, d=dst:
                   E.dma_start(out=d[ocd * 128:(ocd + 1) * 128,
                                     tb * TBW:(tb + 1) * TBW], in_=s),
                   inc="stg", dma=True)
            spst_busy[sl] = p.val("stg")

        def proj(wsrc, n_oc, evac, rhs=None, i32=False, tbs=None):
            rhs = schunk if rhs is None else rhs
            tbs = list(range(NTB)) if tbs is None else tbs
            for oc in range(n_oc):
                buf = wbf_a if oc % 2 == 0 else wbf_b
                p.wait("sync", "mm")
                (wload_T_i32 if i32 else wload_T_f32)(wsrc, oc, buf)
                p.wait("tensor", "evg")
                for i, tb in enumerate(tbs):
                    ps = pE if i % 2 == 0 else pF
                    pwait(ps)
                    for kc in range(NCH):
                        MM(lambda ps=ps: ps[:, 0:512],
                           lambda buf=buf, kc=kc:
                           buf[:, kc * 128:(kc + 1) * 128],
                           lambda kc=kc, tb=tb: rhs(kc, tb),
                           kc == 0, kc == NCH - 1,
                           inc="mm" if kc == NCH - 1 else None)
                    evac(oc, tb, ps)

        def vproj(dst, vwsrc):
            for kc in range(NCH):
                for hh in range(2):
                    p.wait("sync", "evv")
                    p.emit("sync", lambda E, kc=kc, hh=hh: E.dma_start(
                        out=wsf[:, 0:512],
                        in_=vwsrc[kc * 128:(kc + 1) * 128,
                                  hh * 512:(hh + 1) * 512]), inc="ldw",
                        dma=True)
                    p.wait("vector", "ldw")
                    p.emit("vector", lambda E, kc=kc, hh=hh: E.tensor_copy(
                        attb[:, kc * 1024 + hh * 512:
                             kc * 1024 + (hh + 1) * 512], wsf[:, 0:512]),
                        inc="evv")
            p.wait("tensor", "evv")
            vbusy = {}
            for tt in range(32):
                ps = pA if tt % 2 == 0 else pB
                pwait(ps)
                for kc in range(NCH):
                    for n2 in range(2):
                        MM(lambda ps=ps, n2=n2:
                           ps[:, n2 * 512:(n2 + 1) * 512],
                           lambda kc=kc, tt=tt: slot[
                               :, kc * T + tt * 128: kc * T + (tt + 1) * 128],
                           lambda kc=kc, n2=n2: attb[
                               :, kc * 1024 + n2 * 512:
                               kc * 1024 + (n2 + 1) * 512],
                           kc == 0, kc == NCH - 1,
                           inc="mm" if (kc == NCH - 1 and n2 == 1) else None)
                p.wait("vector", "mm")
                if 0 in vbusy:
                    p.wait("vector", "stg", vbusy[0])
                src = ps[:].rearrange("p (h d) -> p h d", h=NH)
                dv = vstg[:].rearrange("p (h e) -> p h e", h=NH)[:, :, 0:64]
                p.emit("vector", lambda E, src=src, dv=dv: E.tensor_copy(
                    dv, src), inc="evv")
                pmark(ps, "evv")
                p.wait("gpsimd", "evv")
                p.emit("gpsimd", lambda E, tt=tt: E.dma_start(
                    out=dst[tt * 128:(tt + 1) * 128, :], in_=vstg[:]),
                    inc="stg", dma=True)
                vbusy[0] = p.val("stg")

        if phases >= 2:
            def ev0(oc, tb, ps):
                evac_spill(qspill if oc < 8 else kspill,
                           bstore[:, oc:oc + 1], oc % 8, tb, ps)
            proj(wqkv0T, 16, ev0)
            for rc in range(H0 // 128):
                for hh in range(4):
                    for qq in range(2):
                        p.wait("sync", "evg")
                        p.wait("sync", "stg")
                        p.emit("sync", lambda E, rc=rc, hh=hh, qq=qq:
                               E.dma_start(
                                   out=wsi[:, 0:128],
                                   in_=wd0T[rc * 128:(rc + 1) * 128,
                                            hh * 256 + qq * 128:
                                            hh * 256 + (qq + 1) * 128]),
                               inc="ldw", dma=True)
                        p.wait("gpsimd", "ldw")
                        p.emit("gpsimd", lambda E, qq=qq: E.tensor_copy(
                            wbf_c[:, qq * 128:(qq + 1) * 128], wsi[:, 0:128]),
                            inc="evg")
                    p.wait("gpsimd", "evg")
                    p.emit("gpsimd", lambda E, rc=rc, hh=hh: E.dma_start(
                        out=wdbf0[rc * 128:(rc + 1) * 128,
                                  hh * 256:(hh + 1) * 256],
                        in_=wbf_c[:, 0:256]), inc="stg", dma=True)
            vproj(vspill, wqkv0T[:, 2048:3072])

        h_busy = {}

        def ffn_gateup(layer):
            nhc = (H0 if layer == 0 else H1) // 128
            for hc in range(nhc):
                p.wait("sync", "mm")
                if layer == 0:
                    wload_T_i32(wg0T, hc, wbf_a)
                    wload_T_i32(wu0T, hc, wbf_b)
                    p.wait("tensor", "evg")
                else:
                    p.emit("sync", lambda E, hc=hc: E.dma_start(
                        out=wbf_a[:, 0:1024],
                        in_=cwtg[hc * 128:(hc + 1) * 128, :]), inc="ldw",
                        dma=True)
                    p.emit("sync", lambda E, hc=hc: E.dma_start(
                        out=wbf_b[:, 0:1024],
                        in_=cwtu[hc * 128:(hc + 1) * 128, :]), inc="ldw",
                        dma=True)
                    p.wait("tensor", "ldw")
                for tb in range(NTB):
                    psg = pC if tb % 2 == 0 else pE
                    psu = pD if tb % 2 == 0 else pF
                    pwait(psg)
                    pwait(psu)
                    for kc in range(NCH):
                        MM(lambda psg=psg: psg[:, 0:512],
                           lambda kc=kc: wbf_a[:, kc * 128:(kc + 1) * 128],
                           lambda kc=kc, tb=tb: schunk(kc, tb),
                           kc == 0, kc == NCH - 1)
                        MM(lambda psu=psu: psu[:, 0:512],
                           lambda kc=kc: wbf_b[:, kc * 128:(kc + 1) * 128],
                           lambda kc=kc, tb=tb: schunk(kc, tb),
                           kc == 0, kc == NCH - 1,
                           inc="mm" if kc == NCH - 1 else None)
                    p.wait("scalar", "mm")
                    p.wait("scalar", "evv")
                    if layer == 0:
                        p.emit("scalar", lambda E, psg=psg: E.activation(
                            sstage[:, 0:512], psg[:, 0:512], AF.Silu,
                            scale=Q127), inc="evs")
                        p.emit("scalar", lambda E, psu=psu: E.activation(
                            sstage[:, 512:1024], psu[:, 0:512], AF.Copy,
                            scale=Q127), inc="evs")
                    else:
                        p.emit("scalar", lambda E, psg=psg, hc=hc:
                               E.activation(
                                   sstage[:, 0:512], psg[:, 0:512], AF.Silu,
                                   scale=wsfm[:, hc:hc + 1]), inc="evs")
                        p.emit("scalar", lambda E, psu=psu, hc=hc:
                               E.activation(
                                   sstage[:, 512:1024], psu[:, 0:512], AF.Copy,
                                   scale=wsfm[:, 22 + hc:23 + hc]), inc="evs")
                    pmark(psg, "evs")
                    pmark(psu, "evs")
                    p.wait("vector", "evs")
                    sl = tb % 2
                    if (layer, sl) in h_busy:
                        p.wait("vector", "stg", h_busy[(layer, sl)])
                    if layer == 0:
                        hsl = spst[:, sl * 512:(sl + 1) * 512]
                        p.emit("vector", lambda E, h=hsl: E.tensor_mul(
                            h, sstage[:, 0:512], sstage[:, 512:1024]), inc="evv")
                        p.wait("gpsimd", "evv")
                        p.emit("gpsimd", lambda E, hc=hc, tb=tb, h=hsl:
                               E.dma_start(
                                   out=h0sp[hc * 128:(hc + 1) * 128,
                                            tb * TBW:(tb + 1) * TBW],
                                   in_=h), inc="stg", dma=True)
                    else:
                        sl = tb % 2
                        hsl = hstage[:, sl * 512:(sl + 1) * 512]
                        p.emit("vector", lambda E, h=hsl: E.tensor_mul(
                            h, sstage[:, 0:512], sstage[:, 512:1024]), inc="evv")
                        p.emit("vector", lambda E, h=hsl: E.tensor_reduce(
                            vtt[:, 1:2], h, AX.X, ALU.max,
                            apply_absolute_value=True), inc="evv")
                        if hc == 0 and tb == 0:
                            p.emit("vector", lambda E: E.tensor_copy(
                                amax2[:, 0:1], vtt[:, 1:2]), inc="evv")
                        else:
                            p.emit("vector", lambda E: E.tensor_max(
                                amax2[:, 0:1], amax2[:, 0:1], vtt[:, 1:2]),
                                inc="evv")
                        p.wait("gpsimd", "evv")
                        p.emit("gpsimd", lambda E, hc=hc, tb=tb, h=hsl:
                               E.dma_start(
                                   out=h1sp[hc * 128:(hc + 1) * 128,
                                            tb * TBW:(tb + 1) * TBW],
                                   in_=h), inc="stg", dma=True)
                    h_busy[(layer, sl)] = p.val("stg")

        def ffn_down(layer):
            nhc = (H0 if layer == 0 else H1) // 128
            wsrc = wdbf0 if layer == 0 else cwtd
            pss = [pA[:, 0:512], pA[:, 512:1024], pB[:, 0:512],
                   pB[:, 512:1024], pC[:, 0:512], pD[:, 0:512],
                   pE[:, 0:512], pF[:, 0:512]]
            hsp_done = p.val("stg")
            ybusy = {}
            for ps in (pA, pB, pC, pD, pE, pF):
                pwait(ps)
            p.wait("sync", "stg", hsp_done)
            p.wait("gpsimd", "stg", hsp_done)
            for tb in range(NTB):
                p.wait("tensor", "evv")
                kcv = {}
                for kc in range(nhc):
                    wb = wbf_a if kc % 2 == 0 else wbf_b
                    if kc >= 2:
                        p.wait("sync", "mm", kcv[kc - 2])
                    else:
                        p.wait("sync", "mm")
                    p.emit("sync", lambda E, kc=kc, wb=wb: E.dma_start(
                        out=wb[:, 0:1024],
                        in_=wsrc[kc * 128:(kc + 1) * 128, :]), inc="ldw",
                        dma=True)
                    if layer == 0:
                        if kc >= 1:
                            p.wait("gpsimd", "mm", kcv[kc - 1])
                        else:
                            p.wait("gpsimd", "mm")
                        p.emit("gpsimd", lambda E, kc=kc, tb=tb: E.dma_start(
                            out=attb[:, 0:512],
                            in_=h0sp[kc * 128:(kc + 1) * 128,
                                     tb * TBW:(tb + 1) * TBW]),
                            inc="ldg", dma=True)
                        p.wait("tensor", "ldg")
                    else:
                        p.wait("gpsimd", "evv")
                        p.emit("gpsimd", lambda E, kc=kc, tb=tb: E.dma_start(
                            out=hstage[:, 0:512],
                            in_=h1sp[kc * 128:(kc + 1) * 128,
                                     tb * TBW:(tb + 1) * TBW]),
                            inc="ldg", dma=True)
                        p.wait("vector", "ldg")
                        if kc >= 1:
                            p.wait("vector", "mm", kcv[kc - 1])
                        else:
                            p.wait("vector", "mm")
                        p.emit("vector", lambda E: E.tensor_scalar(
                            sstage[:, 0:512], hstage[:, 0:512],
                            wsfm[:, 53:54], MAGIC, ALU.mult, ALU.add),
                            inc="evv")
                        p.emit("vector", lambda E: E.tensor_scalar(
                            sstage[:, 0:512], sstage[:, 0:512], -MAGIC,
                            -128.0, ALU.add, ALU.max), inc="evv")
                        p.emit("vector", lambda E: E.tensor_scalar(
                            attb[:, 0:512], sstage[:, 0:512], 127.0, None,
                            ALU.min), inc="evv")
                        p.wait("tensor", "evv")
                    p.wait("tensor", "ldw")
                    for oc in range(8):
                        MM(lambda oc=oc: pss[oc],
                           lambda wb=wb, oc=oc:
                           wb[:, oc * 128:(oc + 1) * 128],
                           lambda: attb[:, 0:512],
                           kc == 0, kc == nhc - 1,
                           inc="mm" if oc == 7 else None)
                    kcv[kc] = p.val("mm")
                p.wait("vector", "mm")
                for oc in range(8):
                    if layer == 0:
                        p.emit("vector", lambda E, oc=oc, tb=tb:
                               E.scalar_tensor_tensor(
                                   rchunk(oc, tb), pss[oc], Q127,
                                   rchunk(oc, tb), ALU.mult, ALU.add),
                               inc="evv")
                    else:
                        p.wait("vector", "stw", ybusy.get(0, 0))
                        p.emit("vector", lambda E, oc=oc, tb=tb:
                               E.scalar_tensor_tensor(
                                   hstage[:, 512:1024],
                                   pss[oc], wsfm[:, 44 + oc:45 + oc],
                                   rchunk(oc, tb), ALU.mult, ALU.add),
                               inc="evv")
                        p.wait("sync", "evv")
                        p.emit("sync", lambda E, oc=oc, tb=tb:
                               E.dma_start(
                                   out=yT[oc * 128:(oc + 1) * 128,
                                          tb * TBW:(tb + 1) * TBW],
                                   in_=hstage[:, 512:1024]),
                               inc="stw", dma=True)
                        ybusy[0] = p.val("stw")
                for ps in (pA, pB, pC, pD, pE, pF):
                    pmark(ps, "evv")

        if phases >= 3:
            ffn_gateup(0)
            ffn_down(0)
            if debug >= 1:
                p.wait("sync", "evv")
                for c in range(NCH):
                    p.emit("sync", lambda E, c=c: E.dma_start(
                        out=dbg["dbg_resid1"][c * 128:(c + 1) * 128, :],
                        in_=resid[:, c * T:(c + 1) * T]), inc="stw", dma=True)

        def wprep():
            jobs = [(wg1n, cwtg, H1 // 128, D, 0),
                    (wu1n, cwtu, H1 // 128, D, 22),
                    (wd1n, cwtd, NCH, H1, 44)]
            for src_, dst, nrc, width, col0 in jobs:
                npieces = (width + 255) // 256
                for rc in range(nrc):
                    for pi in range(npieces):
                        w0 = pi * 256
                        w1 = min(w0 + 256, width)
                        p.wait("sync", "evv")
                        p.wait("sync", "evg")
                        p.emit("sync", lambda E, src_=src_, rc=rc, w0=w0,
                               w1=w1: E.dma_start(
                                   out=wsf[:, 0:w1 - w0],
                                   in_=src_[rc * 128:(rc + 1) * 128, w0:w1]),
                               inc="ldw", dma=True)
                        p.wait("vector", "ldw")
                        p.emit("vector", lambda E, w0=w0, w1=w1:
                               E.tensor_reduce(
                                   vtt[:, 6:7], wsf[:, 0:w1 - w0], AX.X,
                                   ALU.max, apply_absolute_value=True),
                               inc="evv")
                        if pi == 0:
                            p.emit("vector", lambda E: E.tensor_copy(
                                vtt[:, 2:3], vtt[:, 6:7]), inc="evv")
                        else:
                            p.emit("vector", lambda E: E.tensor_max(
                                vtt[:, 2:3], vtt[:, 2:3], vtt[:, 6:7]),
                                inc="evv")
                    p.emit("vector", lambda E: E.tensor_scalar(
                        vtt[:, 2:3], vtt[:, 2:3], Q127, None, ALU.mult),
                        inc="evv")
                    p.wait("scalar", "evv")
                    p.emit("scalar", lambda E: E.activation(
                        vtt[:, 3:4], vtt[:, 2:3], AF.Ln), inc="evs")
                    p.emit("scalar", lambda E: E.activation(
                        vtt[:, 3:4], vtt[:, 3:4], AF.Exp, scale=-1.0),
                        inc="evs")
                    p.wait("vector", "evs")
                    p.emit("vector", lambda E, col0=col0, rc=rc: E.tensor_copy(
                        wsfm[:, col0 + rc:col0 + rc + 1], vtt[:, 2:3]),
                        inc="evv")
                    for pi in range(npieces):
                        w0 = pi * 256
                        w1 = min(w0 + 256, width)
                        pw = w1 - w0
                        p.wait("sync", "evv")
                        p.emit("sync", lambda E, src_=src_, rc=rc, w0=w0,
                               w1=w1: E.dma_start(
                                   out=wsf[:, 0:w1 - w0],
                                   in_=src_[rc * 128:(rc + 1) * 128, w0:w1]),
                               inc="ldw", dma=True)
                        p.wait("vector", "ldw")
                        p.wait("vector", "mm")
                        p.emit("vector", lambda E, pw=pw: E.tensor_scalar(
                            sstage[:, 0:pw], wsf[:, 0:pw], vtt[:, 3:4],
                            MAGIC, ALU.mult, ALU.add), inc="evv")
                        p.emit("vector", lambda E, pw=pw: E.tensor_scalar(
                            sstage[:, 0:pw], sstage[:, 0:pw], -MAGIC,
                            -128.0, ALU.add, ALU.max), inc="evv")
                        p.emit("vector", lambda E, pw=pw: E.tensor_scalar(
                            wbf_c[:, 0:pw], sstage[:, 0:pw], 127.0,
                            None, ALU.min), inc="evv")
                        nblk = pw // 128
                        for j in range(nblk):
                            jj = (w0 // 128) + j
                            sl = jj % 4
                            if ("wp", sl) in h_busy:
                                p.wait("sync", "stg", h_busy[("wp", sl)])
                            p.emit("sync", lambda E, j=j, sl=sl:
                                   E.dma_start(
                                       out=spst[:, sl * 128:(sl + 1) * 128],
                                       in_=wbf_c[:, j * 128:(j + 1) * 128],
                                       transpose=True), inc="stw", dma=True)
                            if width == D:
                                dstb = dst[rc * 128:(rc + 1) * 128,
                                           jj * 128:(jj + 1) * 128]
                            else:
                                dstb = dst[jj * 128:(jj + 1) * 128,
                                           rc * 128:(rc + 1) * 128]
                            p.wait("gpsimd", "stw")
                            p.emit("gpsimd", lambda E, dstb=dstb, sl=sl:
                                   E.dma_start(
                                       out=dstb,
                                       in_=spst[:, sl * 128:(sl + 1) * 128]),
                                   inc="stg", dma=True)
                            h_busy[("wp", sl)] = p.val("stg")
                        p.wait("vector", "stw")

        if phases >= 4:
            wprep()

        qb_busy = {}

        def attention(layer, s):
            if True:
                p.wait("gpsimd", "stg")
                p.wait("gpsimd", "mm")
                p.wait("gpsimd", "evv")
                for c in range(NCH):
                    p.emit("gpsimd", lambda E, c=c, s=s: E.dma_start(
                        out=slot[:, c * SEQ:(c + 1) * SEQ],
                        in_=kspill[c * 128:(c + 1) * 128,
                                   s * SEQ:(s + 1) * SEQ]), inc="ldg",
                        dma=True)
                vsl = vspill[s * SEQ:(s + 1) * SEQ, :]
                v3 = vsl.rearrange("(kt p) c -> p kt c", p=128)
                vd = slot[:, VOFF:VOFF + 16 * 1040].rearrange(
                    "p (kt c) -> p kt c", kt=16)
                p.emit("gpsimd", lambda E, v3=v3, vd=vd: E.dma_start(
                    out=vd, in_=v3), inc="ldg", dma=True)
                p.wait("tensor", "ldg")
                for qb in range(4):
                    kcs = list(range(16)) if layer == 0 else \
                        list(range(4 * qb + 4))
                    for h in range(NH):
                        par = h % 2
                        p.wait("gpsimd", "mm", qb_busy.get(par, 0))
                        p.wait("gpsimd", "evv", qb_busy.get("v", 0))
                        p.emit("gpsimd", lambda E, h=h, s=s, qb=qb, par=par:
                               E.dma_start(
                                   out=qstg[par * 64:(par + 1) * 64, :],
                                   in_=qspill[h * 64:(h + 1) * 64,
                                              s * SEQ + qb * TBW:
                                              s * SEQ + (qb + 1) * TBW]),
                               inc="ldg", dma=True)
                        p.wait("tensor", "ldg")
                        pwait(pC)
                        npv = 0
                        for g0 in range(0, len(kcs), 2):
                            pair = kcs[g0:g0 + 2]
                            psS = pA if (g0 // 2) % 2 == 0 else pB
                            pwait(psS)
                            for j, kc in enumerate(pair):
                                MM(lambda psS=psS, j=j:
                                   psS[:, j * 512:(j + 1) * 512],
                                   lambda h=h, kc=kc, par=par: slot[
                                       par * 64:(par + 1) * 64,
                                       (h // 2) * SEQ + kc * 128:
                                       (h // 2) * SEQ + (kc + 1) * 128],
                                   lambda par=par: qstg[
                                       par * 64:(par + 1) * 64, :],
                                   True, True,
                                   inc="mm" if j == len(pair) - 1 else None)
                            gw = len(pair) * 512
                            if layer == 0:
                                p.wait("vector", "mm")
                                p.wait("vector", "evs")
                                p.wait("vector", "ldg")
                                for j, kc in enumerate(pair):
                                    d0 = kc * 128 - qb * 512
                                    off = 1920 - d0
                                    p.emit("vector", lambda E, j=j, off=off,
                                           h=h, psS=psS:
                                           E.scalar_tensor_tensor(
                                               sstage[:, j * 512:(j + 1) * 512],
                                               dstrip[:, off:off + 512],
                                               abc8[:, h:h + 1],
                                               psS[:, j * 512:(j + 1) * 512],
                                               ALU.mult, ALU.add), inc="evv")
                                pmark(psS, "evv")
                                p.wait("scalar", "evv")
                                p.wait("scalar", "mm")
                                p.wait("scalar", "ldg")
                                p.emit("scalar", lambda E, gw=gw: E.activation(
                                    expb[:, 0:gw], sstage[:, 0:gw], AF.Exp,
                                    bias=biasC[:, 0:1], scale=0.125),
                                    inc="evs")
                            else:
                                p.wait("scalar", "mm")
                                p.wait("scalar", "ldg")
                                p.emit("scalar", lambda E, psS=psS, gw=gw:
                                       E.activation(
                                           expb[:, 0:gw], psS[:, 0:gw],
                                           AF.Exp, bias=biasC[:, 0:1],
                                           scale=0.125), inc="evs")
                                pmark(psS, "evs")
                                p.wait("vector", "evs")
                                nmask = 0
                                for j, kc in enumerate(pair):
                                    if kc >= 4 * qb:
                                        mo = 384 - 128 * (kc - 4 * qb)
                                        p.emit("vector", lambda E, j=j, mo=mo:
                                               E.tensor_mul(
                                                   expb[:, j * 512:(j + 1) * 512],
                                                   expb[:, j * 512:(j + 1) * 512],
                                                   masks[:, mo:mo + 512]),
                                               inc="evv")
                                        nmask += 1
                                if nmask:
                                    p.wait("tensor", "evv")
                            p.wait("tensor", "evs")
                            for j, kc in enumerate(pair):
                                MM(lambda: pC[0:65, 0:512],
                                   lambda kc=kc, h=h: slot[
                                       :, VOFF + kc * 1040 + h * 65:
                                       VOFF + kc * 1040 + (h + 1) * 65],
                                   lambda j=j: expb[:, j * 512:(j + 1) * 512],
                                   npv == 0 and j == 0,
                                   (g0 + 2 >= len(kcs)) and j == len(pair) - 1,
                                   inc="mm" if ((g0 + 2 >= len(kcs)) and
                                                j == len(pair) - 1) else None)
                            npv += 1
                        p.wait("scalar", "mm")
                        p.emit("scalar", lambda E: E.activation(
                            srow[64:65, 0:512], pC[64:65, 0:512], AF.Ln),
                            inc="evs")
                        p.emit("scalar", lambda E: E.activation(
                            srow[64:65, 0:512], srow[64:65, 0:512], AF.Exp,
                            scale=-1.0), inc="evs")
                        p.wait("tensor", "evs")
                        pwait(pD)
                        MM(lambda: pD[0:64, 0:512],
                           lambda: onesf[64:65, 0:64],
                           lambda: srow[64:65, 0:512], True, True, inc="mm")
                        p.wait("vector", "mm")
                        dst = attb[par * 64:(par + 1) * 64,
                                   (h // 2) * SEQ + qb * TBW:
                                   (h // 2) * SEQ + (qb + 1) * TBW]
                        p.emit("vector", lambda E: E.tensor_copy(
                            qstg[0:64, :], pD[0:64, 0:512]), inc="evv")
                        if par == 0:
                            p.emit("vector", lambda E, dst=dst: E.tensor_mul(
                                dst, pC[0:64, 0:512], qstg[0:64, :]),
                                inc="evv")
                        else:
                            p.emit("vector", lambda E: E.tensor_mul(
                                expb[0:64, 0:512], pC[0:64, 0:512],
                                qstg[0:64, :]), inc="evv")
                            p.wait("gpsimd", "evv")
                            p.emit("gpsimd", lambda E, dst=dst: E.dma_start(
                                out=dst, in_=expb[0:64, 0:512]), inc="ldg",
                                dma=True)
                        pmark(pC, "evv")
                        pmark(pD, "evv")
                        qb_busy[par] = p.val("mm")
                        qb_busy["v"] = p.val("evv")

        def outproj(layer, s):
            wsrc = wo0T if layer == 0 else wo1T
            bvcol0 = 0 if layer == 0 else 8
            bocol0 = 24 if layer == 0 else 56
            p.wait("tensor", "ldg")
            p.wait("tensor", "evv")
            for oc in range(8):
                buf = wbf_a if oc % 2 == 0 else wbf_b
                p.wait("sync", "mm")
                wload_T_f32(wsrc, oc, buf)
                p.wait("tensor", "evg")
                pwait(pD)
                for kc in range(NCH):
                    MM(lambda: pD[0:128, 0:1],
                       lambda buf=buf, kc=kc: buf[:, kc * 128:(kc + 1) * 128],
                       lambda kc=kc: bbf[:, bvcol0 + kc:bvcol0 + kc + 1],
                       kc == 0, kc == NCH - 1,
                       inc="mm" if kc == NCH - 1 else None)
                p.wait("vector", "mm")
                p.emit("vector", lambda E, oc=oc: E.tensor_add(
                    bstore[:, 64 + oc:65 + oc], pD[:, 0:1],
                    bstore[:, bocol0 + oc:bocol0 + oc + 1]), inc="evv")
                pmark(pD, "evv")
                for qb in range(4):
                    ps = pE if qb % 2 == 0 else pF
                    pwait(ps)
                    for kc in range(NCH):
                        MM(lambda ps=ps: ps[:, 0:512],
                           lambda buf=buf, kc=kc:
                           buf[:, kc * 128:(kc + 1) * 128],
                           lambda kc=kc, qb=qb: attb[
                               :, kc * SEQ + qb * TBW:
                               kc * SEQ + (qb + 1) * TBW],
                           kc == 0, kc == NCH - 1,
                           inc="mm" if kc == NCH - 1 else None)
                    p.wait("vector", "mm")
                    tb = s * 4 + qb
                    p.emit("vector", lambda E, oc=oc, tb=tb, ps=ps:
                           E.scalar_tensor_tensor(
                               rchunk(oc, tb), ps[:, 0:512],
                               bstore[:, 64 + oc:65 + oc], rchunk(oc, tb),
                               ALU.add, ALU.add), inc="evv")
                    pmark(ps, "evv")

        def attn_tap(layer, s):
            if debug >= 1:
                nm = "dbg_att0" if layer == 0 else "dbg_att1"
                p.wait("sync", "evv")
                p.wait("sync", "ldg")
                for c in range(NCH):
                    p.emit("sync", lambda E, c=c, s=s, nm=nm: E.dma_start(
                        out=dbg[nm][c * 128:(c + 1) * 128,
                                    s * SEQ:(s + 1) * SEQ],
                        in_=attb[:, c * SEQ:(c + 1) * SEQ]),
                        inc="stw", dma=True)

        if phases >= 5:
            for s in range(2):
                attention(0, s)
                attn_tap(0, s)
                outproj(0, s)
            if debug >= 1:
                p.wait("sync", "evv")
                for c in range(NCH):
                    p.emit("sync", lambda E, c=c: E.dma_start(
                        out=dbg["dbg_resid2"][c * 128:(c + 1) * 128, :],
                        in_=resid[:, c * T:(c + 1) * T]), inc="stw", dma=True)

        def cc_push(amax, ccin):
            p.wait("gpsimd", "evv")
            p.wait("gpsimd", "evs")
            for b in range(4):
                p.emit("gpsimd", lambda E, b=b, amax=amax: E.dma_start(
                    out=srow[0:1, 896 + 32 * b:896 + 32 * (b + 1)],
                    in_=amax[32 * b:32 * (b + 1), 0:1]), inc="ldg", dma=True)
            p.wait("vector", "ldg")
            p.emit("vector", lambda E: E.memset(srow[0:1, 880:888], 0.0),
                   inc="evv")
            p.emit("vector", lambda E: E.tensor_reduce(
                srow[0:1, 880:881], srow[0:1, 896:1024], AX.X, ALU.max),
                inc="evv")
            p.wait("gpsimd", "evv")
            p.emit("gpsimd", lambda E, ccin=ccin: E.dma_start(
                out=ccin, in_=srow[0:1, 880:888]), inc="stg", dma=True)
            p.wait("gpsimd", "stg")

        if phases >= 6:
            for tb in range(NTB):
                xhat_tb(tb, collect_max=True, first=(tb == 0))
            cc_push(amax1, ccin1)
            p.emit("gpsimd", lambda E: E.collective_compute(
                "AllReduce", ALU.max, replica_groups=[list(range(8))],
                ins=[ccin1], outs=[ccout1]), inc="cc")
            if debug >= 1:
                p.wait("sync", "evv")
                for c in range(NCH):
                    p.emit("sync", lambda E, c=c: E.dma_start(
                        out=dbg["dbg_xhat1"][c * 128:(c + 1) * 128, :],
                        in_=slot[:, c * T:(c + 1) * T]), inc="stw", dma=True)

        def rope_proj(wsrc, bcol0, dst):
            RP = 8192
            for oc in range(8):
                buf = wbf_a if oc % 2 == 0 else wbf_b
                p.wait("sync", "mm")
                wload_T_f32(wsrc, oc, buf)
                p.wait("tensor", "evg")
                for tb in range(NTB):
                    ps = pE if tb % 2 == 0 else pF
                    pwait(ps)
                    for kc in range(NCH):
                        MM(lambda ps=ps: ps[:, 0:512],
                           lambda buf=buf, kc=kc:
                           buf[:, kc * 128:(kc + 1) * 128],
                           lambda kc=kc, tb=tb: schunk(kc, tb),
                           kc == 0, kc == NCH - 1,
                           inc="mm" if kc == NCH - 1 else None)
                    p.wait("scalar", "mm")
                    p.wait("scalar", "evv")
                    p.emit("scalar", lambda E, ps=ps, tb=tb, bcol0=bcol0,
                           oc=oc: E.activation(
                               attb[:, RP + tb * 512: RP + (tb + 1) * 512],
                               ps[:, 0:512], AF.Identity,
                               bias=bstore[:, bcol0 + oc:bcol0 + oc + 1]),
                           inc="evs")
                    pmark(ps, "evs")
                rope_busy = {}
                for nb in range(8):
                    pwait(pA)
                    p.wait("tensor", "evs")
                    MM(lambda: pA[:, 0:512],
                       lambda: pswap[:, :],
                       lambda nb=nb: attb[:, RP + nb * 512:
                                          RP + (nb + 1) * 512],
                       True, True, inc="mm")
                    p.wait("vector", "mm")
                    tc0 = (nb * 512) % 2048
                    sl = nb % 2
                    if sl in rope_busy:
                        p.wait("vector", "stg", rope_busy[sl])
                    p.emit("vector", lambda E, nb=nb, tc0=tc0, sl=sl:
                           E.tensor_mul(
                               spst[:, sl * 512:(sl + 1) * 512],
                               attb[:, RP + nb * 512: RP + (nb + 1) * 512],
                               ctab[:, tc0:tc0 + 512]), inc="evv")
                    p.emit("vector", lambda E, tc0=tc0: E.tensor_mul(
                        sstage[:, 0:512], pA[:, 0:512],
                        stab[:, tc0:tc0 + 512]), inc="evv")
                    pmark(pA, "evv")
                    p.emit("vector", lambda E, sl=sl: E.tensor_add(
                        spst[:, sl * 512:(sl + 1) * 512],
                        spst[:, sl * 512:(sl + 1) * 512],
                        sstage[:, 0:512]), inc="evv")
                    p.wait("gpsimd", "evv")
                    p.emit("gpsimd", lambda E, oc=oc, nb=nb, sl=sl, dst=dst:
                           E.dma_start(
                               out=dst[oc * 128:(oc + 1) * 128,
                                       nb * 512:(nb + 1) * 512],
                               in_=spst[:, sl * 512:(sl + 1) * 512]),
                           inc="stg", dma=True)
                    rope_busy[sl] = p.val("stg")

        if phases >= 7:
            rope_proj(wq1T, 32, qspill)
            rope_proj(wk1T, 40, kspill)
            vproj(vspill, wv1T)

        if phases >= 8:
            p.wait("gpsimd", "cc", 1)
            p.wait("gpsimd", "evv")
            p.wait("gpsimd", "evs")
            p.emit("gpsimd", lambda E: E.dma_start(
                out=srow[0:1, 512:520], in_=ccout1), inc="ldg", dma=True)
            p.emit("gpsimd", lambda E: E.dma_start(
                out=srow[0:1, 521:522], in_=srow[32:33, 0:1]), inc="ldg",
                dma=True)
            p.wait("vector", "ldg")
            p.emit("vector", lambda E: E.tensor_reduce(
                srow[0:1, 520:521], srow[0:1, 512:520], AX.X, ALU.max),
                inc="evv")
            p.wait("scalar", "evv")
            p.emit("scalar", lambda E: E.activation(
                srow[0:1, 522:523], srow[0:1, 520:521], AF.Identity,
                scale=srow[0:1, 521:522]), inc="evs")
            p.emit("scalar", lambda E: E.activation(
                srow[0:1, 523:524], srow[0:1, 522:523], AF.Identity,
                scale=Q127), inc="evs")
            p.emit("scalar", lambda E: E.activation(
                srow[0:1, 524:525], srow[0:1, 523:524], AF.Ln), inc="evs")
            p.emit("scalar", lambda E: E.activation(
                srow[0:1, 525:526], srow[0:1, 524:525], AF.Exp, scale=-1.0),
                inc="evs")
            p.wait("tensor", "evs")
            pwait(pD)
            MM(lambda: pD[0:128, 0:1], lambda: onesf[0:1, 0:128],
               lambda: srow[0:1, 525:526], True, True, inc="mm")
            p.wait("vector", "mm")
            p.emit("vector", lambda E: E.tensor_copy(
                wsfm[:, 52:53], pD[:, 0:1]), inc="evv")
            pmark(pD, "evv")
            pwait(pD)
            MM(lambda: pD[0:128, 0:1], lambda: onesf[0:1, 0:128],
               lambda: srow[0:1, 523:524], True, True, inc="mm")
            p.wait("vector", "mm")
            p.emit("vector", lambda E: E.tensor_scalar(
                wsfm[:, 0:44], wsfm[:, 0:44], pD[:, 0:1], None, ALU.mult),
                inc="evv")
            pmark(pD, "evv")
            if debug >= 2:
                p.wait("gpsimd", "evv")
                p.wait("gpsimd", "evs")
                p.emit("gpsimd", lambda E: E.dma_start(
                    out=dbg["dbg_tabs"][3:4, 0:32], in_=srow[0:1, 512:544]),
                    inc="stg", dma=True)
                p.wait("vector", "stg")
            for tb in range(NTB):
                ln_stats(tb)
                bcast_row(0, pC)
                bcast_row(512, pD)
                p.wait("vector", "mm")
                for c in range(NCH):
                    p.emit("vector", lambda E, c=c, tb=tb: E.tensor_sub(
                        sstage[:, 0:512], rchunk(c, tb), pC[:, 0:512]),
                        inc="evv")
                    p.emit("vector", lambda E: E.tensor_mul(
                        sstage[:, 512:1024], sstage[:, 0:512], pD[:, 0:512]),
                        inc="evv")
                    p.emit("vector", lambda E: E.tensor_scalar(
                        sstage[:, 0:512], sstage[:, 512:1024],
                        wsfm[:, 52:53], MAGIC, ALU.mult, ALU.add), inc="evv")
                    p.emit("vector", lambda E: E.tensor_scalar(
                        sstage[:, 0:512], sstage[:, 0:512], -MAGIC, -128.0,
                        ALU.add, ALU.max), inc="evv")
                    p.emit("vector", lambda E, c=c, tb=tb: E.tensor_scalar(
                        schunk(c, tb), sstage[:, 0:512], 127.0, None,
                        ALU.min), inc="evv")
                pmark(pC, "evv")
                pmark(pD, "evv")
            if debug >= 1:
                p.wait("sync", "evv")
                for c in range(NCH):
                    p.emit("sync", lambda E, c=c: E.dma_start(
                        out=dbg["dbg_c1"][c * 128:(c + 1) * 128, :],
                        in_=slot[:, c * T:(c + 1) * T]), inc="stw", dma=True)

        if phases >= 9:
            ffn_gateup(1)
            cc_push(amax2, ccin2)
            p.emit("gpsimd", lambda E: E.collective_compute(
                "AllReduce", ALU.max, replica_groups=[list(range(8))],
                ins=[ccin2], outs=[ccout2]), inc="cc")

        if phases >= 10:
            for s in range(2):
                attention(1, s)
                attn_tap(1, s)
                outproj(1, s)

        if phases >= 12:
            p.wait("gpsimd", "cc", 2)
            p.wait("gpsimd", "evv")
            p.wait("gpsimd", "evs")
            p.emit("gpsimd", lambda E: E.dma_start(
                out=srow[0:1, 512:520], in_=ccout2), inc="ldg", dma=True)
            p.emit("gpsimd", lambda E: E.dma_start(
                out=srow[0:1, 521:522], in_=srow[32:33, 1:2]), inc="ldg",
                dma=True)
            p.wait("vector", "ldg")
            p.emit("vector", lambda E: E.tensor_reduce(
                srow[0:1, 520:521], srow[0:1, 512:520], AX.X, ALU.max),
                inc="evv")
            p.wait("scalar", "evv")
            p.emit("scalar", lambda E: E.activation(
                srow[0:1, 522:523], srow[0:1, 520:521], AF.Identity,
                scale=srow[0:1, 521:522]), inc="evs")
            p.emit("scalar", lambda E: E.activation(
                srow[0:1, 523:524], srow[0:1, 522:523], AF.Identity,
                scale=Q127), inc="evs")
            p.emit("scalar", lambda E: E.activation(
                srow[0:1, 524:525], srow[0:1, 523:524], AF.Ln), inc="evs")
            p.emit("scalar", lambda E: E.activation(
                srow[0:1, 525:526], srow[0:1, 524:525], AF.Exp, scale=-1.0),
                inc="evs")
            p.wait("tensor", "evs")
            pwait(pD)
            MM(lambda: pD[0:128, 0:1], lambda: onesf[0:1, 0:128],
               lambda: srow[0:1, 525:526], True, True, inc="mm")
            p.wait("vector", "mm")
            p.emit("vector", lambda E: E.tensor_copy(
                wsfm[:, 53:54], pD[:, 0:1]), inc="evv")
            pmark(pD, "evv")
            pwait(pD)
            MM(lambda: pD[0:128, 0:1], lambda: onesf[0:1, 0:128],
               lambda: srow[0:1, 523:524], True, True, inc="mm")
            p.wait("vector", "mm")
            p.emit("vector", lambda E: E.tensor_scalar(
                wsfm[:, 44:52], wsfm[:, 44:52], pD[:, 0:1], None, ALU.mult),
                inc="evv")
            pmark(pD, "evv")
            ffn_down(1)

        if debug >= 2:
            p.wait("sync", "evv")
            p.wait("gpsimd", "evv")
            p.emit("gpsimd", lambda E: E.dma_start(
                out=dbg["dbg_tabs"][0:1, 0:1024], in_=srow[0:1, :]),
                inc="stg", dma=True)
            p.emit("gpsimd", lambda E: E.dma_start(
                out=dbg["dbg_tabs"][1:2, 0:1], in_=amax1[0:1, :]), inc="stg",
                dma=True)

            p.emit("gpsimd", lambda E: E.dma_start(
                out=dbg["dbg_tabs"][0:128, 128:160], in_=vtt[:, :]),
                inc="stg", dma=True)
            p.emit("gpsimd", lambda E: E.dma_start(
                out=dbg["dbg_tabs"][0:128, 256:320], in_=wsfm[:, :]),
                inc="stg", dma=True)
            p.wait("sync", "stg")
            p.emit("sync", lambda E: E.dma_start(
                out=dbg["dbg_tabs"][128:256, :], in_=stab[:, :]), inc="stw",
                dma=True)
        p.wait("sync", "stw")
        p.wait("sync", "stg")
        p.wait("sync", "ldg")
        p.wait("sync", "evv")
        p.wait("sync", "evs")
        p.wait("sync", "evg")
        p.wait("sync", "mm")

        p.replay(blk)

    return nc, dbg


_PERM = np.concatenate([h * 64 + np.concatenate(
    [np.arange(0, 64, 2), np.arange(1, 64, 2)]) for h in range(NH)])


def _prep_inputs(inputs):
    a = {}
    qkv_w = inputs["L0_attn_qkv_w"]
    a["wqkv0T"] = np.ascontiguousarray(np.asarray(qkv_w, np.float32).T)
    a["bqkv0"] = np.ascontiguousarray(
        np.asarray(inputs["L0_attn_qkv_b"], np.float32).reshape(24, 128).T)
    a["wo0T"] = np.ascontiguousarray(
        np.asarray(inputs["L0_attn_out_w"], np.float32).T)
    a["bo0"] = np.ascontiguousarray(
        np.asarray(inputs["L0_attn_out_b"], np.float32).reshape(8, 128).T)
    a["alpha0"] = np.ascontiguousarray(
        np.asarray(inputs["L0_attn_decay_alpha"], np.float32).reshape(1, NH))
    a["wg0T"] = np.ascontiguousarray(
        np.asarray(inputs["L0_ffn_w_gate"], np.int32).T)
    a["wu0T"] = np.ascontiguousarray(
        np.asarray(inputs["L0_ffn_w_up"], np.int32).T)
    a["wd0T"] = np.ascontiguousarray(
        np.asarray(inputs["L0_ffn_w_down"], np.int32).T)
    Wq = np.asarray(inputs["L1_attn_Wq"], np.float32)
    Wk = np.asarray(inputs["L1_attn_Wk"], np.float32)
    a["wq1T"] = np.ascontiguousarray(Wq[_PERM].T)
    a["wk1T"] = np.ascontiguousarray(Wk[_PERM].T)
    a["bq1"] = np.ascontiguousarray(
        np.asarray(inputs["L1_attn_bq"], np.float32)[_PERM].reshape(8, 128).T)
    a["bk1"] = np.ascontiguousarray(
        np.asarray(inputs["L1_attn_bk"], np.float32)[_PERM].reshape(8, 128).T)
    a["wv1T"] = np.ascontiguousarray(
        np.asarray(inputs["L1_attn_Wv"], np.float32).T)
    a["bv1"] = np.ascontiguousarray(
        np.asarray(inputs["L1_attn_bv"], np.float32).reshape(8, 128).T)
    a["wo1T"] = np.ascontiguousarray(
        np.asarray(inputs["L1_attn_Wo"], np.float32).T)
    a["bo1"] = np.ascontiguousarray(
        np.asarray(inputs["L1_attn_bo"], np.float32).reshape(8, 128).T)
    a["clipx"] = np.ascontiguousarray(
        np.asarray(inputs["L1_ffn_clip_x"], np.float32).reshape(1, 1))
    a["clipd"] = np.ascontiguousarray(
        np.asarray(inputs["L1_ffn_clip_down"], np.float32).reshape(1, 1))
    a["wg1n"] = np.ascontiguousarray(
        np.asarray(inputs["L1_ffn_w_gate"], np.float32))
    a["wu1n"] = np.ascontiguousarray(
        np.asarray(inputs["L1_ffn_w_up"], np.float32))
    a["wd1n"] = np.ascontiguousarray(
        np.asarray(inputs["L1_ffn_w_down"], np.float32))
    return a


def kernel(**inputs):
    x = np.asarray(inputs["L0_attn_x"], np.float32)
    shared = _prep_inputs(inputs)
    nc, _ = build(debug=0, phases=99)
    in_maps = []
    for c in range(8):
        im = dict(shared)
        im["x0T"] = np.ascontiguousarray(
            x[2 * c:2 * c + 2].reshape(T, D).T)
        in_maps.append(im)
    res = run_bass_kernel_spmd(nc, in_maps, core_ids=list(range(8)))
    out = np.empty((16, SEQ, D), np.float32)
    for c in range(8):
        out[2 * c:2 * c + 2] = np.asarray(
            res.results[c]["yT"]).T.reshape(2, SEQ, D)
    return out
